# version 01
# baseline (speedup 1.0000x reference)
import numpy as np
from contextlib import ExitStack

import concourse.bass as bass
import concourse.mybir as mybir
from concourse.bass_utils import run_bass_kernel_spmd

F32 = mybir.dt.float32
BF16 = mybir.dt.bfloat16
AF = mybir.ActivationFunctionType
ALU = mybir.AluOpType
AX = mybir.AxisListType

NCORES = 8
NSEQ = 4
S = 2048
D = 1024
NT = S // 128
NH = 16
DKH = 64
DC = 256
NE = 8
HF = 4096

PHASED = True

MASK_NEG = -30000.0


class Ctx:

    def __init__(self, nc, es, tag=""):
        self.nc = nc
        self.es = es
        self.tag = tag
        self.sems = {}
        self.cnt = {}

    def sem(self, name):
        if name not in self.sems:
            self.sems[name] = self.es.enter_context(
                self.nc.semaphore(self.tag + name))
            self.cnt[name] = 0
        return self.sems[name]

    def inc(self, name, k=1):
        h = self.sem(name)
        self.cnt[name] += k
        return h, self.cnt[name]

    def val(self, name):
        self.sem(name)
        return self.cnt[name]

    def S(self, name, shape, dtype):
        return self.es.enter_context(
            self.nc.sbuf_tensor(self.tag + name, shape, dtype))

    def P(self, name, shape, dtype=F32):
        return self.es.enter_context(
            self.nc.psum_tensor(self.tag + name, shape, dtype))


def dma(eng, ctx, stream, out, in_, **kw):
    h, v = ctx.inc(stream, 16)
    eng.dma_start(out=out, in_=in_, **kw).then_inc(h, 16)
    return v


def make_ln_bufs(ctx, lo=False):
    b = {
        "sum": ctx.S("lnsum", [128, NT], F32),
        "ssq": ctx.S("lnssq", [128, NT], F32),
        "mean": ctx.S("lnmean", [128, NT], F32),
        "var": ctx.S("lnvar", [128, NT], F32),
        "lnv": ctx.S("lnlnv", [128, NT], F32),
        "rstd": ctx.S("lnrstd", [128, NT], F32),
        "scr": ctx.S("lnscr", [128, 1024], F32),
    }
    if lo:
        b["xf"] = ctx.S("lnxf", [128, 1024], F32)
    return b


def emit_ln(ctx, block, lnb, s_x, s_xn, lx_val, extra_lo=None,
            ac="ac", dv="dv"):
    nc = ctx.nc
    s_sum = lnb["sum"]; s_ssq = lnb["ssq"]; s_mean = lnb["mean"]
    s_var = lnb["var"]; s_lnv = lnb["lnv"]; s_rstd = lnb["rstd"]
    s_scr = lnb["scr"]
    if extra_lo is not None:
        s_xf = lnb["xf"]

    ac_sq0 = ctx.val(ac)
    sqd = [ac_sq0 + t + 1 for t in range(NT)]
    dv_red0 = ctx.val(dv)
    redd = [dv_red0 + 2 * t + 2 for t in range(NT)]

    @block.scalar
    def _(eng):
        eng.wait_ge(ctx.sem("lx"), lx_val)
        for t in range(NT):
            if t >= 1:
                eng.wait_ge(ctx.sem(dv), redd[t - 1])
            h, v = ctx.inc(ac)
            eng.activation(s_scr[:], s_x[:, t, :], AF.Square).then_inc(h, 1)

    @block.vector
    def _(eng):
        eng.wait_ge(ctx.sem("lx"), lx_val)
        for t in range(NT):
            eng.reduce_sum(s_sum[:, t:t + 1], s_x[:, t, :],
                           axis=AX.X).then_inc(ctx.sem(dv), 1)
            ctx.cnt[dv] += 1
            eng.wait_ge(ctx.sem(ac), sqd[t])
            eng.reduce_sum(s_ssq[:, t:t + 1], s_scr[:],
                           axis=AX.X).then_inc(ctx.sem(dv), 1)
            ctx.cnt[dv] += 1
        eng.wait_ge(ctx.sem(dv), ctx.val(dv))
        h, v = ctx.inc(dv)
        eng.tensor_scalar(s_mean[:], s_sum[:], 1.0 / D, None,
                          ALU.mult).then_inc(h, 1)
        h2, v2 = ctx.inc(dv)
        eng.tensor_scalar(s_var[:], s_ssq[:], 1.0 / D, None,
                          ALU.mult).then_inc(h2, 1)
        eng.wait_ge(h2, v2)
        h, v = ctx.inc(dv)
        eng.tensor_tensor(s_lnv[:], s_mean[:], s_mean[:],
                          ALU.mult).then_inc(h, 1)
        eng.wait_ge(h, v)
        h, v = ctx.inc(dv)
        eng.tensor_tensor(s_var[:], s_var[:], s_lnv[:],
                          ALU.subtract).then_inc(h, 1)
        eng.wait_ge(h, v)
        h, v = ctx.inc(dv)
        eng.tensor_scalar(s_var[:], s_var[:], 1e-5, None,
                          ALU.add).then_inc(h, 1)
        ctx.var_done = v

    @block.scalar
    def _(eng):
        eng.wait_ge(ctx.sem(dv), ctx.var_done)
        h, v = ctx.inc(ac)
        eng.activation(s_lnv[:], s_var[:], AF.Ln).then_inc(h, 1)
        eng.wait_ge(h, v)
        h, v = ctx.inc(ac)
        eng.activation(s_rstd[:], s_lnv[:], AF.Exp,
                       scale=-0.5).then_inc(h, 1)
        ctx.rstd_done = v

    @block.vector
    def _(eng):
        eng.wait_ge(ctx.sem(ac), ctx.rstd_done)
        for t in range(NT):
            if extra_lo is None:
                i = eng.tensor_scalar(s_xn[:, t, :], s_x[:, t, :],
                                      s_mean[:, t:t + 1], s_rstd[:, t:t + 1],
                                      ALU.subtract, ALU.mult)
            else:
                h, v = ctx.inc(dv)
                eng.tensor_scalar(s_xf[:], s_x[:, t, :],
                                  s_mean[:, t:t + 1], s_rstd[:, t:t + 1],
                                  ALU.subtract, ALU.mult).then_inc(h, 1)
                eng.wait_ge(h, v)
                h, v = ctx.inc(dv)
                eng.tensor_copy(s_xn[:, t, :], s_xf[:]).then_inc(h, 1)
                eng.wait_ge(h, v)
                i = eng.tensor_tensor(extra_lo[:, t, :], s_xf[:],
                                      s_xn[:, t, :], ALU.subtract)
        h, v = ctx.inc(dv)
        i.then_inc(h, 1)
    return ctx.val(dv)


def emit_transpose_batch(ctx, block, jobs, s_idb, banks, wait=None,
                         pe="pe", dv="dv", evac_engine="vector"):
    nc = ctx.nc
    nbk = len(banks)
    pe_base = ctx.val(pe)
    dv_base = ctx.val(dv)

    @block.tensor
    def _(eng):
        if wait is not None:
            for sname, v in wait:
                eng.wait_ge(ctx.sem(sname), v)
        for k, (src, dst) in enumerate(jobs):
            if k >= nbk:
                eng.wait_ge(ctx.sem(dv), dv_base + k - nbk + 1)
            h, v = ctx.inc(pe)
            eng.transpose(banks[k % nbk][:], src, s_idb).then_inc(h, 1)

    blk = getattr(block, evac_engine)

    @blk
    def _(eng):
        for k, (src, dst) in enumerate(jobs):
            eng.wait_ge(ctx.sem(pe), pe_base + k + 1)
            h, v = ctx.inc(dv)
            if evac_engine == "vector":
                eng.tensor_copy(dst, banks[k % nbk][:]).then_inc(h, 1)
            else:
                eng.activation(dst, banks[k % nbk][:], AF.Copy).then_inc(h, 1)
    return ctx.val(dv)


def head_slice(buf, h, cols):
    return buf[64 * (h % 2):64 * (h % 2) + 64, h // 2, cols]


DEBUG_L0 = False


def emit_l0attn(ctx, block, d_xin, d_W, d_eye, d_mask, d_ones, d_xout,
                d_dbg=None):
    nc = ctx.nc
    if DEBUG_L0:
        s_dbgS = ctx.S("sb_dbgS", [128, 512], BF16)
        s_dbgP = ctx.S("sb_dbgP", [128, 512], BF16)
    s_idb = ctx.S("idb", [128, 128], BF16)
    s_maskb = ctx.S("maskb", [128, 4, 512], BF16)
    s_onesb = ctx.S("onesb", [1, 128], BF16)
    sWdq = ctx.S("Wdq", [128, 8, DC], BF16)
    sWdkv = ctx.S("Wdkv", [128, 8, DC], BF16)
    sWuq = ctx.S("Wuq", [128, 2, D], BF16)
    sWuk = ctx.S("Wuk", [128, 2, D], BF16)
    sWuv = ctx.S("Wuv", [128, 2, D], BF16)
    sWo = ctx.S("Wo", [128, 8, D], BF16)
    s_xt = ctx.S("xt", [128, 2, 1024], F32)
    s_xnp = ctx.S("xnp", [128, 2, 1024], BF16)
    s_sum = ctx.S("csum", [128, NT], F32)
    s_ssq = ctx.S("cssq", [128, NT], F32)
    s_mean = ctx.S("cmean", [128, NT], F32)
    s_var = ctx.S("cvar", [128, NT], F32)
    s_lnv = ctx.S("clnv", [128, NT], F32)
    s_rstd = ctx.S("crstd", [128, NT], F32)
    s_TV = ctx.S("TV", [128, NT * 1040], BF16)
    s_xnT = s_TV[:, 0:8 * S].rearrange("p (c n) -> p c n", c=8)
    s_V = s_TV[:].rearrange("p (t n) -> p t n", t=NT)
    s_cqT = ctx.S("cqT", [128, 2, S], BF16)
    s_ckvT = ctx.S("ckvT", [128, 2, S], BF16)
    s_QT = ctx.S("QT", [128, 8, S], BF16)
    s_KT = ctx.S("KT", [128, 8, S], BF16)
    s_OT = ctx.S("OT", [128, 8, S], BF16)
    s_P = ctx.S("Pt", [128, 2, 512], BF16)
    s_inv = ctx.S("inv", [1, 512], F32)
    s_invh = ctx.S("invh", [1, 512], BF16)
    s_invl = ctx.S("invl", [1, 512], BF16)
    s_invbc = ctx.S("invbc", [64, 512], F32)
    p_S = [ctx.P("pS%d" % i, [128, 512], F32) for i in range(2)]
    p_O = ctx.P("pO", [65, 512], F32)
    p_B = ctx.P("pB", [64, 512], F32)
    p_tr = [ctx.P("ptr%d" % i, [128, 128], BF16) for i in range(4)]

    @block.gpsimd
    def _(eng):
        dma(eng, ctx, "lw", s_idb[:], d_eye)
        dma(eng, ctx, "lw", s_maskb[:], d_mask)
        dma(eng, ctx, "lw", s_onesb[:], d_ones)
        dma(eng, ctx, "lw", sWdq[:],
            d_W["dq"].rearrange("(c p) j -> p c j", p=128))
        dma(eng, ctx, "lw", sWdkv[:],
            d_W["dkv"].rearrange("(c p) j -> p c j", p=128))
        dma(eng, ctx, "lw", sWuq[:],
            d_W["uq"].rearrange("(c p) j -> p c j", p=128))
        dma(eng, ctx, "lw", sWuk[:],
            d_W["uk"].rearrange("(c p) j -> p c j", p=128))
        dma(eng, ctx, "lw", sWuv[:],
            d_W["uv"].rearrange("(c p) j -> p c j", p=128))
        dma(eng, ctx, "lw", sWo[:],
            d_W["o"].rearrange("(c p) j -> p c j", p=128))
        ctx.w_loaded = ctx.val("lw")

    st_prev = 0
    for sq in range(NSEQ):
        xsrc = d_xin[sq].rearrange("(t p) d -> p t d", p=128)
        lx_t = [0] * NT
        ac_base = ctx.val("ac")
        dv_base = ctx.val("dv")
        sq_done = [ac_base + t + 1 for t in range(NT)]
        sum_done = [dv_base + 2 * t + 1 for t in range(NT)]
        ssq_done = [dv_base + 2 * t + 2 for t in range(NT)]
        red_done = sum_done

        @block.sync
        def _(eng):
            eng.wait_ge(ctx.sem("st"), st_prev)
            for t in range(NT):
                if t >= 2:
                    eng.wait_ge(ctx.sem("ac"), sq_done[t - 2])
                    eng.wait_ge(ctx.sem("dv"), sum_done[t - 2])
                lx_t[t] = dma(eng, ctx, "lx", s_xt[:, t % 2, :], xsrc[:, t, :])

        @block.scalar
        def _(eng):
            for t in range(NT):
                eng.wait_ge(ctx.sem("lx"), lx_t[t])
                if t >= 2:
                    eng.wait_ge(ctx.sem("dv"), ssq_done[t - 2])
                h, v = ctx.inc("ac")
                eng.activation(s_xnp[:, t % 2, :], s_xt[:, t % 2, :],
                               AF.Square).then_inc(h, 1)

        @block.vector
        def _(eng):
            for t in range(NT):
                eng.wait_ge(ctx.sem("lx"), lx_t[t])
                eng.reduce_sum(s_sum[:, t:t + 1], s_xt[:, t % 2, :],
                               axis=AX.X).then_inc(ctx.sem("dv"), 1)
                ctx.cnt["dv"] += 1
                eng.wait_ge(ctx.sem("ac"), sq_done[t])
                eng.reduce_sum(s_ssq[:, t:t + 1], s_xnp[:, t % 2, :],
                               axis=AX.X).then_inc(ctx.sem("dv"), 1)
                ctx.cnt["dv"] += 1
            eng.wait_ge(ctx.sem("dv"), ctx.val("dv"))
            h, v = ctx.inc("dv")
            eng.tensor_scalar(s_mean[:], s_sum[:], 1.0 / D, None,
                              ALU.mult).then_inc(h, 1)
            h2, v2 = ctx.inc("dv")
            eng.tensor_scalar(s_var[:], s_ssq[:], 1.0 / D, None,
                              ALU.mult).then_inc(h2, 1)
            eng.wait_ge(h2, v2)
            h, v = ctx.inc("dv")
            eng.tensor_tensor(s_lnv[:], s_mean[:], s_mean[:],
                              ALU.mult).then_inc(h, 1)
            eng.wait_ge(h, v)
            h, v = ctx.inc("dv")
            eng.tensor_tensor(s_var[:], s_var[:], s_lnv[:],
                              ALU.subtract).then_inc(h, 1)
            eng.wait_ge(h, v)
            h, v = ctx.inc("dv")
            eng.tensor_scalar(s_var[:], s_var[:], 1e-5, None,
                              ALU.add).then_inc(h, 1)
            ctx.var_done = v

        @block.scalar
        def _(eng):
            eng.wait_ge(ctx.sem("dv"), ctx.var_done)
            h, v = ctx.inc("ac")
            eng.activation(s_lnv[:], s_var[:], AF.Ln).then_inc(h, 1)
            eng.wait_ge(h, v)
            h, v = ctx.inc("ac")
            eng.activation(s_rstd[:], s_lnv[:], AF.Exp,
                           scale=-0.5).then_inc(h, 1)
            ctx.rstd_done = v

        lxb_t = [0] * NT
        dvb = ctx.val("dv")
        ap_done = [dvb + t + 1 for t in range(NT)]
        pe_b = ctx.val("pe")
        ac_b = ctx.val("ac")

        @block.sync
        def _(eng):
            for t in range(NT):
                if t < 2:
                    eng.wait_ge(ctx.sem("ac"), sq_done[NT - 2 + t])
                    eng.wait_ge(ctx.sem("dv"), red_done[NT - 2 + t])
                else:
                    eng.wait_ge(ctx.sem("dv"), ap_done[t - 2])
                lxb_t[t] = dma(eng, ctx, "lx", s_xt[:, t % 2, :],
                               xsrc[:, t, :])

        @block.vector
        def _(eng):
            eng.wait_ge(ctx.sem("ac"), ctx.rstd_done)
            for t in range(NT):
                eng.wait_ge(ctx.sem("lx"), lxb_t[t])
                if t >= 2:
                    eng.wait_ge(ctx.sem("pe"), pe_b + 8 * (t - 1))
                h, v = ctx.inc("dv")
                eng.tensor_scalar(s_xnp[:, t % 2, :], s_xt[:, t % 2, :],
                                  s_mean[:, t:t + 1], s_rstd[:, t:t + 1],
                                  ALU.subtract, ALU.mult).then_inc(h, 1)


        @block.tensor
        def _(eng):
            k = 0
            for t in range(NT):
                eng.wait_ge(ctx.sem("dv"), ap_done[t])
                for c in range(8):
                    if k >= 4:
                        eng.wait_ge(ctx.sem("ac"), ac_b + k - 3)
                    h, v = ctx.inc("pe")
                    eng.transpose(p_tr[k % 4][:],
                                  s_xnp[:, t % 2, 128 * c:128 * (c + 1)],
                                  s_idb[:]).then_inc(h, 1)
                    k += 1

        @block.scalar
        def _(eng):
            k = 0
            for t in range(NT):
                for c in range(8):
                    eng.wait_ge(ctx.sem("pe"), pe_b + k + 1)
                    h, v = ctx.inc("ac")
                    eng.activation(s_xnT[:, c, 128 * t:128 * (t + 1)],
                                   p_tr[k % 4][:], AF.Copy).then_inc(h, 1)
                    k += 1
            ctx.xnT_done = ctx.val("ac")

        def proj(dst, lhs_buf, lhs_nc, rhs_buf, n_m, wait_ac, ones_fill=False):
            pe_b2 = ctx.val("pe")
            ac_b2 = ctx.val("ac")
            k = 0
            jobs = []
            for m in range(n_m):
                for a in range(4):
                    jobs.append((m, a))

            @block.tensor
            def _(eng):
                eng.wait_ge(ctx.sem("ac"), wait_ac)
                kk = 0
                for m, a in jobs:
                    if kk >= 2:
                        eng.wait_ge(ctx.sem("ac"), ac_b2 + kk - 1)
                    for c in range(lhs_nc):
                        i = eng.matmul(p_S[kk % 2][:],
                                       lhs_buf[:, c, 128 * m:128 * (m + 1)],
                                       rhs_buf[:, c, 512 * a:512 * (a + 1)],
                                       start=(c == 0), stop=(c == lhs_nc - 1))
                    h, v = ctx.inc("pe")
                    i.then_inc(h, 1)
                    kk += 1

            @block.scalar
            def _(eng):
                kk = 0
                for m, a in jobs:
                    eng.wait_ge(ctx.sem("pe"), pe_b2 + kk + 1)
                    h, v = ctx.inc("ac")
                    eng.activation(dst[:, m, 512 * a:512 * (a + 1)],
                                   p_S[kk % 2][:], AF.Copy).then_inc(h, 1)
                    kk += 1
            return ctx.val("ac")

        cq_done = proj(s_cqT, sWdq, 8, s_xnT, 2, ctx.xnT_done)
        ckv_done = proj(s_ckvT, sWdkv, 8, s_xnT, 2, cq_done)
        q_done = proj(s_QT, sWuq, 2, s_cqT, 8, ckv_done)
        k_done = proj(s_KT, sWuk, 2, s_ckvT, 8, q_done)

        h, v = ctx.inc("dv")
        ctx.vones_done = v

        @block.vector
        def _(eng):
            eng.wait_ge(ctx.sem("ac"), k_done)
            eng.memset(s_V[:], 1.0).then_inc(ctx.sems["dv"], 1)

        pe_bV = ctx.val("pe")
        ac_bV = ctx.val("ac")

        @block.tensor
        def _(eng):
            eng.wait_ge(ctx.sem("dv"), ctx.vones_done)
            kk = 0
            for t in range(NT):
                for v2 in range(2):
                    if kk >= 2:
                        eng.wait_ge(ctx.sem("ac"), ac_bV + kk - 1)
                    for c in range(2):
                        i = eng.matmul(p_S[kk % 2][:],
                                       s_ckvT[:, c, 128 * t:128 * (t + 1)],
                                       sWuv[:, c, 512 * v2:512 * (v2 + 1)],
                                       start=(c == 0), stop=(c == 1))
                    h, v = ctx.inc("pe")
                    i.then_inc(h, 1)
                    kk += 1

        @block.scalar
        def _(eng):
            kk = 0
            for t in range(NT):
                for v2 in range(2):
                    eng.wait_ge(ctx.sem("pe"), pe_bV + kk + 1)
                    dst = s_V[:, t, :].rearrange("p (h c) -> p h c", h=NT)[
                        :, 8 * v2:8 * (v2 + 1), 0:64]
                    h, v = ctx.inc("ac")
                    eng.activation(dst, p_S[kk % 2][:].rearrange(
                        "p (h c) -> p h c", h=8), AF.Copy).then_inc(h, 1)
                    kk += 1
            ctx.v_done = ctx.val("ac")

        ot_done_prev = 0
        sem_pe = ctx.sem("pe")
        sem_ac = ctx.sem("ac")
        sem_dv = ctx.sem("dv")
        for hh in range(NH):
            for a in range(4):
                jmax = 4 * a + 3
                nj = jmax + 1
                pe_b = ctx.val("pe")
                ac_b = ctx.val("ac")
                dv_b = ctx.val("dv")
                smm = {j: pe_b + 2 * j + 1 for j in range(nj)}
                av = {j: pe_b + 2 * j + 2 for j in range(nj)}
                bc_mm = pe_b + 2 * nj + 1
                expd = {j: ac_b + j + 1 for j in range(nj)}
                maskd = {j: dv_b + (j - 4 * a) + 1
                         for j in range(4 * a, nj)}
                recip_d = dv_b + 5
                invh_d = dv_b + 6
                invl_d = dv_b + 7
                invbc_d = dv_b + 8
                ot_d = dv_b + 9
                for _ in range(2 * nj + 1):
                    ctx.inc("pe")
                for _ in range(nj):
                    ctx.inc("ac")
                for _ in range(9):
                    ctx.inc("dv")

                @block.tensor
                def _(eng, hh=hh, a=a, jmax=jmax, smm=smm, av=av,
                      expd=expd, maskd=maskd, invl_d=invl_d,
                      invbc_prev=ot_done_prev, bc_mm=bc_mm):
                    if (hh, a) == (0, 0):
                        eng.wait_ge(sem_ac, ctx.v_done)
                    for j in range(jmax + 1):
                        if j >= 2:
                            eng.wait_ge(sem_ac, expd[j - 2])
                        eng.matmul(p_S[j % 2][:],
                                   head_slice(s_KT, hh,
                                              slice(128 * j, 128 * (j + 1))),
                                   head_slice(s_QT, hh,
                                              slice(512 * a, 512 * (a + 1))),
                                   start=True, stop=True).then_inc(sem_pe, 1)
                        if j >= 4 * a:
                            eng.wait_ge(sem_dv, maskd[j])
                        else:
                            eng.wait_ge(sem_ac, expd[j])
                        if j == 0 and invbc_prev:
                            eng.wait_ge(sem_dv, invbc_prev)
                        eng.matmul(p_O[:], s_V[:, j, 65 * hh:65 * hh + 65],
                                   s_P[:, j % 2, :], start=(j == 0),
                                   stop=(j == jmax)).then_inc(sem_pe, 1)
                    eng.wait_ge(sem_dv, invl_d)
                    eng.matmul(p_B[:], s_onesb[0:1, 0:64], s_invh[:],
                               start=True, stop=False)
                    eng.matmul(p_B[:], s_onesb[0:1, 0:64], s_invl[:],
                               start=False, stop=True).then_inc(sem_pe, 1)

                if DEBUG_L0 and (hh, a) == (0, 0):
                    @block.vector
                    def _(eng, smm=smm):
                        eng.wait_ge(sem_pe, smm[0])
                        h2, v2 = ctx.inc("dbg")
                        eng.tensor_copy(s_dbgS[:], p_S[0][:]).then_inc(h2, 1)

                @block.scalar
                def _(eng, hh=hh, a=a, jmax=jmax, smm=smm, av=av):
                    for j in range(jmax + 1):
                        eng.wait_ge(sem_pe, smm[j])
                        if j >= 2:
                            eng.wait_ge(sem_pe, av[j - 2])
                        if DEBUG_L0 and (hh, a) == (0, 0) and j == 0:
                            eng.wait_ge(ctx.sem("dbg"), 1)
                        eng.activation(s_P[:, j % 2, :], p_S[j % 2][:],
                                       AF.Exp, scale=0.125).then_inc(sem_ac, 1)

                @block.vector
                def _(eng, hh=hh, a=a, jmax=jmax, av=av, expd=expd,
                      bc_mm=bc_mm):
                    for j in range(4 * a, jmax + 1):
                        if DEBUG_L0 and (hh, a) == (0, 0) and j == 0:
                            eng.wait_ge(sem_ac, expd[j])
                            eng.tensor_tensor(s_P[:, 0, :], s_P[:, 0, :],
                                              s_maskb[:, 0, :], ALU.mult)
                            h2, v2 = ctx.inc("dbg")
                            eng.tensor_copy(s_dbgP[:],
                                            s_P[:, 0, :]).then_inc(h2, 1)
                            eng.wait_ge(h2, v2)
                            eng.nop().then_inc(sem_dv, 1)
                            continue
                        eng.wait_ge(sem_ac, expd[j])
                        eng.tensor_tensor(s_P[:, j % 2, :], s_P[:, j % 2, :],
                                          s_maskb[:, j - 4 * a, :],
                                          ALU.mult).then_inc(sem_dv, 1)
                    eng.wait_ge(sem_pe, av[jmax])
                    eng.reciprocal(s_inv[:], p_O[64:65, :]).then_inc(sem_dv, 1)
                    eng.wait_ge(sem_dv, recip_d)
                    eng.tensor_copy(s_invh[:], s_inv[:]).then_inc(sem_dv, 1)
                    eng.wait_ge(sem_dv, invh_d)
                    eng.tensor_tensor(s_invl[:], s_inv[:], s_invh[:],
                                      ALU.subtract).then_inc(sem_dv, 1)
                    eng.wait_ge(sem_pe, bc_mm)
                    eng.tensor_copy(s_invbc[:], p_B[:]).then_inc(sem_dv, 1)
                    eng.wait_ge(sem_dv, invbc_d)
                    eng.tensor_tensor(
                        head_slice(s_OT, hh, slice(512 * a, 512 * (a + 1))),
                        p_O[0:64, :], s_invbc[:], ALU.mult).then_inc(sem_dv, 1)
                ot_done_prev = ot_d
                assert ctx.val("dv") == ot_d
                assert ctx.val("pe") == bc_mm

        lxr_t = [0] * NT
        res_done = [0] * NT
        st_t = [0] * NT
        pe_bW = ctx.val("pe") + 128 * NT // 16 * 0
        dv_bW = ctx.val("dv")
        for t in range(NT):
            res_done[t] = dv_bW + 2 * (t + 1)

        @block.sync
        def _(eng):
            for t in range(NT + 2):
                if t >= 2:
                    eng.wait_ge(ctx.sem("dv"), res_done[t - 2])
                    st_t[t - 2] = dma(
                        eng, ctx, "st",
                        d_xout[sq].rearrange("(t p) d -> p t d",
                                             p=128)[:, t - 2, :],
                        s_xt[:, t % 2, :])
                if t < NT:
                    if t >= 2:
                        eng.wait_ge(ctx.sem("st"), st_t[t - 2])
                    lxr_t[t] = dma(eng, ctx, "lx", s_xt[:, t % 2, :],
                                   xsrc[:, t, :])

        pe_bW = ctx.val("pe")

        @block.tensor
        def _(eng):
            eng.wait_ge(ctx.sem("dv"), ot_done_prev)
            kk = 0
            for t in range(NT):
                for o2 in range(2):
                    if kk >= 2:
                        eng.wait_ge(ctx.sem("dv"), dv_bW + kk - 1)
                    for c in range(8):
                        i = eng.matmul(p_S[kk % 2][:],
                                       s_OT[:, c, 128 * t:128 * (t + 1)],
                                       sWo[:, c, 512 * o2:512 * (o2 + 1)],
                                       start=(c == 0), stop=(c == 7))
                    h, v = ctx.inc("pe")
                    i.then_inc(h, 1)
                    kk += 1

        @block.vector
        def _(eng):
            kk = 0
            for t in range(NT):
                eng.wait_ge(ctx.sem("lx"), lxr_t[t])
                for o2 in range(2):
                    eng.wait_ge(ctx.sem("pe"), pe_bW + kk + 1)
                    h, v = ctx.inc("dv")
                    eng.tensor_tensor(s_xt[:, t % 2, 512 * o2:512 * (o2 + 1)],
                                      s_xt[:, t % 2, 512 * o2:512 * (o2 + 1)],
                                      p_S[kk % 2][:], ALU.add).then_inc(h, 1)
                    kk += 1
        st_prev = ctx.val("st")

    @block.sync
    def _(eng):
        eng.wait_ge(ctx.sem("st"), ctx.val("st"))
        if DEBUG_L0:
            eng.wait_ge(ctx.sem("dbg"), 2)
            dma(eng, ctx, "st", d_dbg[0], s_dbgS[:])
            dma(eng, ctx, "st", d_dbg[1], s_dbgP[:])
            dma(eng, ctx, "st", d_dbg[2], s_QT[:])
            dma(eng, ctx, "st", d_dbg[3], s_cqT[:])
            dma(eng, ctx, "st", d_dbg[4], s_KT[:])
            eng.wait_ge(ctx.sem("st"), ctx.val("st"))


def build_l0attn():
    nc = bass.Bass("TRN2", target_bir_lowering=False)
    d_xin = nc.dram_tensor("xin", [NSEQ, S, D], F32, kind="ExternalInput").ap()
    d_W = {
        "dq": nc.dram_tensor("wdq", [D, DC], F32, kind="ExternalInput").ap(),
        "uq": nc.dram_tensor("wuq", [DC, D], F32, kind="ExternalInput").ap(),
        "dkv": nc.dram_tensor("wdkv", [D, DC], F32, kind="ExternalInput").ap(),
        "uk": nc.dram_tensor("wuk", [DC, D], F32, kind="ExternalInput").ap(),
        "uv": nc.dram_tensor("wuv", [DC, D], F32, kind="ExternalInput").ap(),
        "o": nc.dram_tensor("wo", [D, D], F32, kind="ExternalInput").ap(),
    }
    d_eye = nc.dram_tensor("eye", [128, 128], F32, kind="ExternalInput").ap()
    d_mask = nc.dram_tensor("mask", [128, 4, 512], F32,
                            kind="ExternalInput").ap()
    d_ones = nc.dram_tensor("ones", [1, 128], F32, kind="ExternalInput").ap()
    d_xout = nc.dram_tensor("xout", [NSEQ, S, D], F32,
                            kind="ExternalOutput").ap()
    d_dbg = None
    if DEBUG_L0:
        d_dbg = [nc.dram_tensor("dbgS", [128, 512], BF16,
                                kind="ExternalOutput").ap(),
                 nc.dram_tensor("dbgP", [128, 512], BF16,
                                kind="ExternalOutput").ap(),
                 nc.dram_tensor("dbgQ", [128, 8, S], BF16,
                                kind="ExternalOutput").ap(),
                 nc.dram_tensor("dbgCQ", [128, 2, S], BF16,
                                kind="ExternalOutput").ap(),
                 nc.dram_tensor("dbgK", [128, 8, S], BF16,
                                kind="ExternalOutput").ap()]
    es = ExitStack()
    ctx = Ctx(nc, es)
    block = es.enter_context(nc.Block())
    emit_l0attn(ctx, block, d_xin, d_W, d_eye, d_mask, d_ones, d_xout, d_dbg)
    es.close()
    return nc


def make_masks():
    m = np.zeros((4, 128, 512), dtype=np.float32)
    for r in range(4):
        kk = np.arange(128)[:, None] + 128 * r
        qq = np.arange(512)[None, :]
        m[r] = (kk <= qq).astype(np.float32)
    return np.ascontiguousarray(m.transpose(1, 0, 2))


ONES_ROW = np.ones((1, 128), dtype=np.float32)


DEBUG_L1 = False


def emit_l1attn(ctx, block, d_xin, d_W, d_eye, d_ones, d_xout, d_dbg=None):
    nc = ctx.nc
    if DEBUG_L1:
        d_dbgSB = ctx.S("sb_dbgSB", [128, 512], F32)[:]
    s_idb = ctx.S("idb", [128, 128], BF16)
    s_onesb = ctx.S("onesb", [1, 128], BF16)
    sW = ctx.S("Wqkv", [128, 8, D], BF16)
    s_xt = ctx.S("xt", [128, 2, 1024], F32)
    s_xnp = ctx.S("xnp", [128, 2, 1024], BF16)
    s_sum = ctx.S("csum", [128, NT], F32)
    s_ssq = ctx.S("cssq", [128, NT], F32)
    s_mean = ctx.S("cmean", [128, NT], F32)
    s_var = ctx.S("cvar", [128, NT], F32)
    s_lnv = ctx.S("clnv", [128, NT], F32)
    s_rstd = ctx.S("crstd", [128, NT], F32)
    s_xnT = ctx.S("xnT", [128, 8, S], BF16)
    s_V = ctx.S("Vt", [128, NT, 1040], BF16)
    s_QT = ctx.S("QT", [128, 8, S], BF16)
    s_KT = ctx.S("KT", [128, 8, S], BF16)
    s_OT = ctx.S("OT", [128, 8, S], BF16)
    s_P = ctx.S("Pt", [128, 2, 512], BF16)
    s_nm2 = ctx.S("nm2", [128, NT, 2], F32)
    s_nm = ctx.S("nm", [128, NT], F32)
    s_nmsh = ctx.S("nmsh", [128, NT], F32)
    s_cb = ctx.S("cb", [1, 1], BF16)
    s_ccol = ctx.S("ccol", [128, 1], F32)
    s_nmb = ctx.S("nmb", [128, NT], BF16)
    s_nmTb = ctx.S("nmTb", [16, 128], BF16)
    s_nmrowb = ctx.S("nmrowb", [1, S], BF16)
    s_sum8 = ctx.S("sum8", [1, 512], F32)
    s_inv = ctx.S("inv", [1, 512], F32)
    s_invh = ctx.S("invh", [1, 512], BF16)
    s_invl = ctx.S("invl", [1, 512], BF16)
    s_invbc = ctx.S("invbc", [64, 512], F32)
    p_W = [ctx.P("pW%d" % i, [128, 1024], F32) for i in range(2)]
    p_S = [p_W[0][:, 0:512], p_W[1][:, 0:512]]
    p_O = ctx.P("pO", [65, 512], F32)
    p_B = ctx.P("pB", [64, 512], F32)
    p_tr = [ctx.P("ptr%d" % i, [128, 128], BF16) for i in range(2)]

    @block.gpsimd
    def _(eng):
        dma(eng, ctx, "lw", s_idb[:], d_eye)
        dma(eng, ctx, "lw", s_onesb[:], d_ones)
        ctx.w_loaded = ctx.val("lw")

    def load_w(nm, wait_ac):
        @block.gpsimd
        def _(eng):
            eng.wait_ge(ctx.sem("ac"), wait_ac)
            for c in range(8):
                dma(eng, ctx, "lw",
                    sW[:, c, :].rearrange("p (h e) -> p h e", h=NH),
                    d_W[nm][:, 128 * c:128 * (c + 1), :].rearrange(
                        "h p e -> p h e"))
        return ctx.val("lw")

    st_prev = 0
    for sq in range(NSEQ):
        xsrc = d_xin[sq].rearrange("(t p) d -> p t d", p=128)
        lx_t = [0] * NT
        ac_base = ctx.val("ac")
        dv_base = ctx.val("dv")
        sq_done = [ac_base + t + 1 for t in range(NT)]
        sum_done = [dv_base + 2 * t + 1 for t in range(NT)]
        ssq_done = [dv_base + 2 * t + 2 for t in range(NT)]
        red_done = sum_done

        @block.sync
        def _(eng):
            eng.wait_ge(ctx.sem("st"), st_prev)
            for t in range(NT):
                if t >= 2:
                    eng.wait_ge(ctx.sem("ac"), sq_done[t - 2])
                    eng.wait_ge(ctx.sem("dv"), sum_done[t - 2])
                lx_t[t] = dma(eng, ctx, "lx", s_xt[:, t % 2, :], xsrc[:, t, :])

        @block.scalar
        def _(eng):
            for t in range(NT):
                eng.wait_ge(ctx.sem("lx"), lx_t[t])
                if t >= 2:
                    eng.wait_ge(ctx.sem("dv"), ssq_done[t - 2])
                h, v = ctx.inc("ac")
                eng.activation(s_xnp[:, t % 2, :], s_xt[:, t % 2, :],
                               AF.Square).then_inc(h, 1)

        @block.vector
        def _(eng):
            for t in range(NT):
                eng.wait_ge(ctx.sem("lx"), lx_t[t])
                eng.reduce_sum(s_sum[:, t:t + 1], s_xt[:, t % 2, :],
                               axis=AX.X).then_inc(ctx.sem("dv"), 1)
                ctx.cnt["dv"] += 1
                eng.wait_ge(ctx.sem("ac"), sq_done[t])
                eng.reduce_sum(s_ssq[:, t:t + 1], s_xnp[:, t % 2, :],
                               axis=AX.X).then_inc(ctx.sem("dv"), 1)
                ctx.cnt["dv"] += 1
            eng.wait_ge(ctx.sem("dv"), ctx.val("dv"))
            h, v = ctx.inc("dv")
            eng.tensor_scalar(s_mean[:], s_sum[:], 1.0 / D, None,
                              ALU.mult).then_inc(h, 1)
            h2, v2 = ctx.inc("dv")
            eng.tensor_scalar(s_var[:], s_ssq[:], 1.0 / D, None,
                              ALU.mult).then_inc(h2, 1)
            eng.wait_ge(h2, v2)
            h, v = ctx.inc("dv")
            eng.tensor_tensor(s_lnv[:], s_mean[:], s_mean[:],
                              ALU.mult).then_inc(h, 1)
            eng.wait_ge(h, v)
            h, v = ctx.inc("dv")
            eng.tensor_tensor(s_var[:], s_var[:], s_lnv[:],
                              ALU.subtract).then_inc(h, 1)
            eng.wait_ge(h, v)
            h, v = ctx.inc("dv")
            eng.tensor_scalar(s_var[:], s_var[:], 1e-5, None,
                              ALU.add).then_inc(h, 1)
            ctx.var_done = v

        @block.scalar
        def _(eng):
            eng.wait_ge(ctx.sem("dv"), ctx.var_done)
            h, v = ctx.inc("ac")
            eng.activation(s_lnv[:], s_var[:], AF.Ln).then_inc(h, 1)
            eng.wait_ge(h, v)
            h, v = ctx.inc("ac")
            eng.activation(s_rstd[:], s_lnv[:], AF.Exp,
                           scale=-0.5).then_inc(h, 1)
            ctx.rstd_done = v

        lxb_t = [0] * NT
        dvb = ctx.val("dv")
        ap_done = [dvb + t + 1 for t in range(NT)]
        pe_b = ctx.val("pe")
        ac_b = ctx.val("ac")

        @block.sync
        def _(eng):
            for t in range(NT):
                if t < 2:
                    eng.wait_ge(ctx.sem("ac"), sq_done[NT - 2 + t])
                    eng.wait_ge(ctx.sem("dv"), red_done[NT - 2 + t])
                else:
                    eng.wait_ge(ctx.sem("dv"), ap_done[t - 2])
                lxb_t[t] = dma(eng, ctx, "lx", s_xt[:, t % 2, :],
                               xsrc[:, t, :])

        @block.vector
        def _(eng):
            eng.wait_ge(ctx.sem("ac"), ctx.rstd_done)
            for t in range(NT):
                eng.wait_ge(ctx.sem("lx"), lxb_t[t])
                if t >= 2:
                    eng.wait_ge(ctx.sem("pe"), pe_b + 8 * (t - 1))
                h, v = ctx.inc("dv")
                eng.tensor_scalar(s_xnp[:, t % 2, :], s_xt[:, t % 2, :],
                                  s_mean[:, t:t + 1], s_rstd[:, t:t + 1],
                                  ALU.subtract, ALU.mult).then_inc(h, 1)

        @block.tensor
        def _(eng):
            if sq == 0:
                eng.wait_ge(ctx.sem("lw"), ctx.w_loaded)
            k = 0
            for t in range(NT):
                eng.wait_ge(ctx.sem("dv"), ap_done[t])
                for c in range(8):
                    if k >= 2:
                        eng.wait_ge(ctx.sem("ac"), ac_b + k - 1)
                    h, v = ctx.inc("pe")
                    eng.transpose(p_tr[k % 2][:],
                                  s_xnp[:, t % 2, 128 * c:128 * (c + 1)],
                                  s_idb[:]).then_inc(h, 1)
                    k += 1

        @block.scalar
        def _(eng):
            k = 0
            for t in range(NT):
                for c in range(8):
                    eng.wait_ge(ctx.sem("pe"), pe_b + k + 1)
                    h, v = ctx.inc("ac")
                    eng.activation(s_xnT[:, c, 128 * t:128 * (t + 1)],
                                   p_tr[k % 2][:], AF.Copy).then_inc(h, 1)
                    k += 1
            ctx.xnT_done = ctx.val("ac")

        def proj_f(dst, wait_ac, wait_lw):
            pe_b2 = ctx.val("pe")
            ac_b2 = ctx.val("ac")
            wbuf = sW

            @block.tensor
            def _(eng):
                eng.wait_ge(ctx.sem("ac"), wait_ac)
                eng.wait_ge(ctx.sem("lw"), wait_lw)
                kk = 0
                for m in range(8):
                    for a in range(4):
                        if kk >= 2:
                            eng.wait_ge(ctx.sem("ac"), ac_b2 + kk - 1)
                        for c in range(8):
                            i = eng.matmul(p_S[kk % 2],
                                           wbuf[:, c, 128 * m:128 * (m + 1)],
                                           s_xnT[:, c, 512 * a:512 * (a + 1)],
                                           start=(c == 0), stop=(c == 7))
                        h, v = ctx.inc("pe")
                        i.then_inc(h, 1)
                        kk += 1

            @block.scalar
            def _(eng):
                kk = 0
                for m in range(8):
                    for a in range(4):
                        eng.wait_ge(ctx.sem("pe"), pe_b2 + kk + 1)
                        h, v = ctx.inc("ac")
                        eng.activation(dst[:, m, 512 * a:512 * (a + 1)],
                                       p_S[kk % 2], AF.Copy).then_inc(h, 1)
                        kk += 1
            return ctx.val("ac")

        wq_lw = load_w("q", ctx.val("ac") if sq == 0 else ctx.prev_vdone)
        q_done = proj_f(s_QT, ctx.xnT_done, wq_lw)
        wk_lw = load_w("k", q_done)
        k_done = proj_f(s_KT, q_done, wk_lw)
        wv_lw = load_w("v", k_done)

        h, vones = ctx.inc("dv")

        @block.vector
        def _(eng):
            eng.wait_ge(ctx.sem("ac"), k_done)
            eng.memset(s_V[:], 1.0).then_inc(ctx.sems["dv"], 1)

        pe_bV = ctx.val("pe")
        ac_bV = ctx.val("ac")

        @block.tensor
        def _(eng):
            eng.wait_ge(ctx.sem("dv"), vones)
            eng.wait_ge(ctx.sem("lw"), wv_lw)
            kk = 0
            for t in range(NT):
                for v2 in range(2):
                    if kk >= 2:
                        eng.wait_ge(ctx.sem("ac"), ac_bV + kk - 1)
                    for c in range(8):
                        i = eng.matmul(p_S[kk % 2],
                                       s_xnT[:, c, 128 * t:128 * (t + 1)],
                                       sW[:, c, 512 * v2:512 * (v2 + 1)],
                                       start=(c == 0), stop=(c == 7))
                    h, v = ctx.inc("pe")
                    i.then_inc(h, 1)
                    kk += 1

        @block.scalar
        def _(eng):
            kk = 0
            for t in range(NT):
                for v2 in range(2):
                    eng.wait_ge(ctx.sem("pe"), pe_bV + kk + 1)
                    dst = s_V[:, t, :].rearrange("p (h c) -> p h c", h=NT)[
                        :, 8 * v2:8 * (v2 + 1), 0:64]
                    h, v = ctx.inc("ac")
                    eng.activation(dst, p_S[kk % 2].rearrange(
                        "p (h c) -> p h c", h=8), AF.Copy).then_inc(h, 1)
                    kk += 1
            ctx.v_done = ctx.val("ac")
            ctx.prev_vdone = ctx.v_done

        ot_done_prev = 0
        sem_pe = ctx.sem("pe")
        sem_ac = ctx.sem("ac")
        sem_dv = ctx.sem("dv")
        sem_fl = ctx.sem("fl")
        for hh in range(NH):
            pe_b1 = ctx.val("pe")
            dv_b1 = ctx.val("dv")
            mm1 = {}
            red1 = {}
            kk = 0
            for i in range(NT):
                for half in range(2):
                    mm1[(i, half)] = pe_b1 + kk + 1
                    red1[(i, half)] = dv_b1 + kk + 1
                    kk += 1
            for _ in range(32):
                ctx.inc("pe")
                ctx.inc("dv")

            @block.tensor
            def _(eng, hh=hh, mm1=mm1, red1=red1):
                if hh == 0:
                    eng.wait_ge(sem_ac, ctx.v_done)
                for i in range(NT):
                    for half in range(2):
                        if (i, half) != (0, 0) or hh != 0:
                            kprev = 2 * i + half - 2
                            if kprev >= 0:
                                eng.wait_ge(sem_dv, dv_b1 + kprev + 1)
                            elif hh > 0:
                                pass
                        for g in range(2):
                            i2 = eng.matmul(
                                p_W[half][:, 512 * g:512 * (g + 1)],
                                head_slice(s_QT, hh,
                                           slice(128 * i, 128 * (i + 1))),
                                head_slice(s_KT, hh,
                                           slice(1024 * half + 512 * g,
                                                 1024 * half + 512 * (g + 1))),
                                start=True, stop=True)
                        i2.then_inc(sem_pe, 1)

            @block.vector
            def _(eng, hh=hh, mm1=mm1, red1=red1):
                for i in range(NT):
                    for half in range(2):
                        eng.wait_ge(sem_pe, mm1[(i, half)])
                        eng.reduce_max(s_nm2[:, i, half:half + 1],
                                       p_W[half][:], axis=AX.X,
                                       negate=True).then_inc(sem_dv, 1)
                eng.wait_ge(sem_dv, red1[(NT - 1, 1)])
                h, v = ctx.inc("dv")
                eng.tensor_tensor(s_nm[:], s_nm2[:, :, 0], s_nm2[:, :, 1],
                                  ALU.min).then_inc(h, 1)
                ctx.nm_done = v

            @block.vector
            def _(eng, hh=hh):
                eng.wait_ge(sem_dv, ctx.nm_done)
                h, v = ctx.inc("dv")
                eng.tensor_copy(s_cb[:], s_nm[0:1, 0:1]).then_inc(h, 1)
                ctx.cb_done = v

            @block.tensor
            def _(eng, hh=hh):
                eng.wait_ge(sem_dv, ctx.cb_done)
                h, v = ctx.inc("pe")
                eng.matmul(p_W[0][:, 0:1], s_onesb[:], s_cb[:],
                           start=True, stop=True).then_inc(h, 1)
                ctx.cbc_pe = v

            @block.vector
            def _(eng, hh=hh):
                eng.wait_ge(sem_pe, ctx.cbc_pe)
                h, v = ctx.inc("dv")
                eng.tensor_copy(s_ccol[:], p_W[0][:, 0:1]).then_inc(h, 1)
                eng.wait_ge(h, v)
                h, v = ctx.inc("dv")
                eng.tensor_scalar(s_nmsh[:], s_nm[:], s_ccol[0:128, 0:1],
                                  None, ALU.subtract).then_inc(h, 1)
                eng.wait_ge(h, v)
                h, v = ctx.inc("dv")
                eng.tensor_copy(s_nmb[:], s_nmsh[:]).then_inc(h, 1)
                ctx.nmb_done = v

            @block.tensor
            def _(eng, hh=hh):
                eng.wait_ge(sem_dv, ctx.nmb_done)
                h, v = ctx.inc("pe")
                eng.transpose(p_tr[0][0:16, 0:128], s_nmb[:],
                              s_idb[:]).then_inc(h, 1)
                ctx.nmtr_pe = v

            @block.vector
            def _(eng, hh=hh):
                eng.wait_ge(sem_pe, ctx.nmtr_pe)
                h, v = ctx.inc("dv")
                eng.tensor_copy(s_nmTb[:], p_tr[0][0:16, 0:128]).then_inc(h, 1)
                ctx.nmT_done = v

            @block.sync
            def _(eng, hh=hh):
                eng.wait_ge(sem_dv, ctx.nmT_done)
                ctx.nmrowb_done_fl = dma(eng, ctx, "fl", s_nmrowb[0:1, :],
                                         s_nmTb[:])

            @block.vector
            def _(eng, hh=hh):
                eng.wait_ge(sem_fl, ctx.nmrowb_done_fl)
                h, v = ctx.inc("dv")
                eng.memset(s_nm2[0:1, 0:1, 0:1], 0.0).then_inc(h, 1)
                ctx.nmrowb_done = v

            plan = []
            for a in range(4):
                pj = {}
                for j in range(NT):
                    _, v1 = ctx.inc("pe")
                    _, v2 = ctx.inc("pe")
                    _, ve = ctx.inc("ac")
                    pj[j] = (v1, v2, ve)
                _, bcv = ctx.inc("pe")
                dvs = [ctx.inc("dv")[1] for _ in range(6)]
                plan.append((pj, bcv, dvs))

            if DEBUG_L1 and hh == 0:
                @block.vector
                def _(eng):
                    eng.wait_ge(sem_pe, plan[0][0][0][0])
                    h2, v2 = ctx.inc("dbg")
                    eng.tensor_copy(d_dbgSB, p_S[0]).then_inc(h2, 1)
                ctx.dbg_sb = ctx.val("dbg")

            @block.tensor
            def _(eng, hh=hh, plan=plan):
                for a in range(4):
                    pj, bcv, dvs = plan[a]
                    for j in range(NT):
                        v1, v2, ve = pj[j]
                        if j >= 2:
                            eng.wait_ge(sem_ac, pj[j - 2][2])
                        elif a > 0:
                            eng.wait_ge(sem_ac, plan[a - 1][0][NT - 2 + j][2])
                        eng.wait_ge(sem_dv, ctx.nmrowb_done)
                        eng.matmul(p_S[j % 2], s_onesb[:],
                                   s_nmrowb[0:1, 512 * a:512 * (a + 1)],
                                   start=True, stop=False)
                        eng.matmul(p_S[j % 2],
                                   head_slice(s_KT, hh,
                                              slice(128 * j, 128 * (j + 1))),
                                   head_slice(s_QT, hh,
                                              slice(512 * a, 512 * (a + 1))),
                                   start=False, stop=True).then_inc(sem_pe, 1)
                        eng.wait_ge(sem_ac, ve)
                        if j == 0:
                            if a == 0:
                                if ot_done_prev:
                                    eng.wait_ge(sem_dv, ot_done_prev)
                            else:
                                eng.wait_ge(sem_dv, plan[a - 1][2][5])
                        eng.matmul(p_O[:], s_V[:, j, 65 * hh:65 * hh + 65],
                                   s_P[:, j % 2, :], start=(j == 0),
                                   stop=(j == NT - 1)).then_inc(sem_pe, 1)
                    eng.wait_ge(sem_dv, dvs[3])
                    eng.matmul(p_B[:], s_onesb[0:1, 0:64], s_invh[:],
                               start=True, stop=False)
                    eng.matmul(p_B[:], s_onesb[0:1, 0:64], s_invl[:],
                               start=False, stop=True).then_inc(sem_pe, 1)

            @block.scalar
            def _(eng, hh=hh, plan=plan):
                for a in range(4):
                    pj, bcv, dvs = plan[a]
                    for j in range(NT):
                        v1, v2, ve = pj[j]
                        eng.wait_ge(sem_pe, v1)
                        if j >= 2:
                            eng.wait_ge(sem_pe, pj[j - 2][1])
                        if DEBUG_L1 and hh == 0 and a == 0 and j == 0:
                            eng.wait_ge(ctx.sem("dbg"), ctx.dbg_sb)
                        eng.activation(s_P[:, j % 2, :], p_S[j % 2],
                                       AF.Exp,
                                       bias=s_ccol[:, 0:1]).then_inc(sem_ac, 1)

            @block.vector
            def _(eng, hh=hh, plan=plan):
                for a in range(4):
                    pj, bcv, dvs = plan[a]
                    sum8_d, recip_d, invh_d, invl_d, invbc_d, ot_d = dvs
                    eng.wait_ge(sem_pe, pj[NT - 1][1])
                    eng.tensor_scalar(s_sum8[:], p_O[64:65, :], 8.0, None,
                                      ALU.mult).then_inc(sem_dv, 1)
                    eng.wait_ge(sem_dv, sum8_d)
                    eng.reciprocal(s_inv[:], s_sum8[:]).then_inc(sem_dv, 1)
                    eng.wait_ge(sem_dv, recip_d)
                    eng.tensor_copy(s_invh[:], s_inv[:]).then_inc(sem_dv, 1)
                    eng.wait_ge(sem_dv, invh_d)
                    eng.tensor_tensor(s_invl[:], s_inv[:], s_invh[:],
                                      ALU.subtract).then_inc(sem_dv, 1)
                    eng.wait_ge(sem_pe, bcv)
                    eng.tensor_copy(s_invbc[:], p_B[:]).then_inc(sem_dv, 1)
                    eng.wait_ge(sem_dv, invbc_d)
                    eng.tensor_tensor(
                        head_slice(s_OT, hh, slice(512 * a, 512 * (a + 1))),
                        p_O[0:64, :], s_invbc[:], ALU.mult).then_inc(sem_dv, 1)
            ot_done_prev = plan[3][2][5]
            assert ctx.val("dv") == ot_done_prev

        lxr_t = [0] * NT
        st_t = [0] * NT
        dv_bR = ctx.val("dv")
        res_done = [dv_bR + 8 * (t + 1) for t in range(NT)]
        pe_bR = ctx.val("pe")

        @block.sync
        def _(eng):
            for t in range(NT + 2):
                if t >= 2:
                    eng.wait_ge(ctx.sem("dv"), res_done[t - 2])
                    st_t[t - 2] = dma(
                        eng, ctx, "st",
                        d_xout[sq].rearrange("(t p) d -> p t d",
                                             p=128)[:, t - 2, :],
                        s_xt[:, t % 2, :])
                if t < NT:
                    if t >= 2:
                        eng.wait_ge(ctx.sem("st"), st_t[t - 2])
                    lxr_t[t] = dma(eng, ctx, "lx", s_xt[:, t % 2, :],
                                   xsrc[:, t, :])

        @block.tensor
        def _(eng):
            eng.wait_ge(ctx.sem("dv"), ot_done_prev)
            k = 0
            for t in range(NT):
                for c in range(8):
                    if k >= 2:
                        eng.wait_ge(ctx.sem("dv"), dv_bR + k - 1)
                    h, v = ctx.inc("pe")
                    eng.transpose(p_tr[k % 2][:],
                                  s_OT[:, c, 128 * t:128 * (t + 1)],
                                  s_idb[:]).then_inc(h, 1)
                    k += 1

        @block.vector
        def _(eng):
            k = 0
            for t in range(NT):
                eng.wait_ge(ctx.sem("lx"), lxr_t[t])
                for c in range(8):
                    eng.wait_ge(ctx.sem("pe"), pe_bR + k + 1)
                    h, v = ctx.inc("dv")
                    eng.tensor_tensor(
                        s_xt[:, t % 2, 128 * c:128 * (c + 1)],
                        s_xt[:, t % 2, 128 * c:128 * (c + 1)],
                        p_tr[k % 2][:], ALU.add).then_inc(h, 1)
                    k += 1
        st_prev = ctx.val("st")

    @block.sync
    def _(eng):
        eng.wait_ge(ctx.sem("st"), ctx.val("st"))
        if DEBUG_L1:
            dma(eng, ctx, "st", d_dbg[0], s_QT[:])
            dma(eng, ctx, "st", d_dbg[1], s_KT[:])
            dma(eng, ctx, "st", d_dbg[2], s_nmrowb[:])
            dma(eng, ctx, "st", d_dbg[3], s_OT[:])
            dma(eng, ctx, "st", d_dbg[5], s_V[:].rearrange("p t n -> p (t n)"))
            dma(eng, ctx, "st", d_dbg[6], s_ccol[:].rearrange("p o -> p o"))
            dma(eng, ctx, "st", d_dbg[7], d_dbgSB)
            dma(eng, ctx, "st", d_dbg[4], s_nm[:])
            eng.wait_ge(ctx.sem("st"), ctx.val("st"))


def build_l1attn():
    nc = bass.Bass("TRN2", target_bir_lowering=False)
    d_xin = nc.dram_tensor("xin", [NSEQ, S, D], F32, kind="ExternalInput").ap()
    d_W = {
        "q": nc.dram_tensor("wq", [NH, D, DKH], F32, kind="ExternalInput").ap(),
        "k": nc.dram_tensor("wk", [NH, D, DKH], F32, kind="ExternalInput").ap(),
        "v": nc.dram_tensor("wv", [NH, D, DKH], F32, kind="ExternalInput").ap(),
    }
    d_eye = nc.dram_tensor("eye", [128, 128], F32, kind="ExternalInput").ap()
    d_ones = nc.dram_tensor("ones", [1, 128], F32, kind="ExternalInput").ap()
    d_xout = nc.dram_tensor("xout", [NSEQ, S, D], F32,
                            kind="ExternalOutput").ap()
    d_dbg = None
    if DEBUG_L1:
        d_dbg = [nc.dram_tensor("dbgQ", [128, 8, S], BF16,
                                kind="ExternalOutput").ap(),
                 nc.dram_tensor("dbgK", [128, 8, S], BF16,
                                kind="ExternalOutput").ap(),
                 nc.dram_tensor("dbgNM", [1, S], BF16,
                                kind="ExternalOutput").ap(),
                 nc.dram_tensor("dbgOT", [128, 8, S], BF16,
                                kind="ExternalOutput").ap(),
                 nc.dram_tensor("dbgNMc", [128, NT], F32,
                                kind="ExternalOutput").ap(),
                 nc.dram_tensor("dbgV", [128, NT * 1040], BF16,
                                kind="ExternalOutput").ap(),
                 nc.dram_tensor("dbgInv", [128, 1], F32,
                                kind="ExternalOutput").ap(),
                 nc.dram_tensor("dbgSB", [128, 512], F32,
                                kind="ExternalOutput").ap()]
    es = ExitStack()
    ctx = Ctx(nc, es)
    block = es.enter_context(nc.Block())
    emit_l1attn(ctx, block, d_xin, d_W, d_eye, d_ones, d_xout, d_dbg)
    es.close()
    return nc


def emit_l1moe(ctx, block, d_xin, d_W, d_eye, d_xout, d_xnT, d_acc):
    nc = ctx.nc
    s_idb = ctx.S("idb", [128, 128], BF16)
    s_g = ctx.S("g", [128, NSEQ, NT, NE], F32)
    s_rw = ctx.S("rws", [128, 8, NE], F32)
    s_rwh = ctx.S("rwh", [128, 8, NE], BF16)
    s_rwl = ctx.S("rwl", [128, 8, NE], BF16)
    s_rbb = ctx.S("rbbs", [128, NE], F32)

    es_ln = ExitStack()
    ctx_es_saved = ctx.es
    ctx.es = es_ln
    s_xt = ctx.S("xt", [128, 2, 1024], F32)
    s_xf = ctx.S("xf", [128, 1024], F32)
    s_xnp = ctx.S("xnp", [128, 2, 1024], BF16)
    s_xlp = ctx.S("xlp", [128, 2, 1024], BF16)
    s_sum = ctx.S("csum", [128, NT], F32)
    s_ssq = ctx.S("cssq", [128, NT], F32)
    s_mean = ctx.S("cmean", [128, NT], F32)
    s_var = ctx.S("cvar", [128, NT], F32)
    s_lnv = ctx.S("clnv", [128, NT], F32)
    s_rstd = ctx.S("crstd", [128, NT], F32)
    s_xnT = ctx.S("xnT", [128, 8, S], BF16)
    s_xloT = ctx.S("xloT", [128, 2, 8, 128], BF16)
    s_lg = ctx.S("lgs", [128, NE], F32)
    s_m1 = ctx.S("m1", [128, 1], F32)
    s_nm1 = ctx.S("nm1", [128, 1], F32)
    s_meq = ctx.S("meq", [128, NE], F32)
    s_lg2 = ctx.S("lg2", [128, NE], F32)
    s_m2 = ctx.S("m2", [128, 1], F32)
    s_keep = ctx.S("keep", [128, NE], F32)
    s_ex = ctx.S("ex", [128, NE], F32)
    s_sm = ctx.S("sm", [128, 1], F32)
    s_iv = ctx.S("iv", [128, 1], F32)
    p_tr = [ctx.P("ptr%d" % i, [128, 128], BF16) for i in range(2)]
    p_r = ctx.P("pr", [128, NE], F32)
    ctx.es = ctx_es_saved

    @block.gpsimd
    def _(eng):
        dma(eng, ctx, "lw", s_idb[:], d_eye)
        dma(eng, ctx, "lw", s_rw[:], d_W["rw"].rearrange(
            "(c p) e -> p c e", p=128))
        dma(eng, ctx, "lw", s_rwh[:], d_W["rw"].rearrange(
            "(c p) e -> p c e", p=128))
        dma(eng, ctx, "lw", s_rbb[:], d_W["rbb"])
        ctx.w_loaded = ctx.val("lw")

    @block.vector
    def _(eng):
        eng.wait_ge(ctx.sem("lw"), ctx.w_loaded)
        h, v = ctx.inc("dv")
        eng.tensor_tensor(s_rwl[:], s_rw[:], s_rwh[:],
                          ALU.subtract).then_inc(h, 1)
        ctx.rw_done = v

    st_prev = 0
    for sq in range(NSEQ):
        xsrc = d_xin[sq].rearrange("(t p) d -> p t d", p=128)
        lx_t = [0] * NT
        ac_base = ctx.val("ac")
        dv_base = ctx.val("dv")
        sq_done = [ac_base + t + 1 for t in range(NT)]
        sum_done = [dv_base + 2 * t + 1 for t in range(NT)]
        ssq_done = [dv_base + 2 * t + 2 for t in range(NT)]

        @block.sync
        def _(eng):
            eng.wait_ge(ctx.sem("st"), st_prev)
            for t in range(NT):
                if t >= 2:
                    eng.wait_ge(ctx.sem("ac"), sq_done[t - 2])
                    eng.wait_ge(ctx.sem("dv"), sum_done[t - 2])
                lx_t[t] = dma(eng, ctx, "lx", s_xt[:, t % 2, :], xsrc[:, t, :])

        @block.scalar
        def _(eng):
            for t in range(NT):
                eng.wait_ge(ctx.sem("lx"), lx_t[t])
                if t >= 2:
                    eng.wait_ge(ctx.sem("dv"), ssq_done[t - 2])
                h, v = ctx.inc("ac")
                eng.activation(s_xnp[:, t % 2, :], s_xt[:, t % 2, :],
                               AF.Square).then_inc(h, 1)

        @block.vector
        def _(eng):
            for t in range(NT):
                eng.wait_ge(ctx.sem("lx"), lx_t[t])
                eng.reduce_sum(s_sum[:, t:t + 1], s_xt[:, t % 2, :],
                               axis=AX.X).then_inc(ctx.sem("dv"), 1)
                ctx.cnt["dv"] += 1
                eng.wait_ge(ctx.sem("ac"), sq_done[t])
                eng.reduce_sum(s_ssq[:, t:t + 1], s_xnp[:, t % 2, :],
                               axis=AX.X).then_inc(ctx.sem("dv"), 1)
                ctx.cnt["dv"] += 1
            eng.wait_ge(ctx.sem("dv"), ctx.val("dv"))
            h, v = ctx.inc("dv")
            eng.tensor_scalar(s_mean[:], s_sum[:], 1.0 / D, None,
                              ALU.mult).then_inc(h, 1)
            h2, v2 = ctx.inc("dv")
            eng.tensor_scalar(s_var[:], s_ssq[:], 1.0 / D, None,
                              ALU.mult).then_inc(h2, 1)
            eng.wait_ge(h2, v2)
            h, v = ctx.inc("dv")
            eng.tensor_tensor(s_lnv[:], s_mean[:], s_mean[:],
                              ALU.mult).then_inc(h, 1)
            eng.wait_ge(h, v)
            h, v = ctx.inc("dv")
            eng.tensor_tensor(s_var[:], s_var[:], s_lnv[:],
                              ALU.subtract).then_inc(h, 1)
            eng.wait_ge(h, v)
            h, v = ctx.inc("dv")
            eng.tensor_scalar(s_var[:], s_var[:], 1e-5, None,
                              ALU.add).then_inc(h, 1)
            ctx.var_done = v

        @block.scalar
        def _(eng):
            eng.wait_ge(ctx.sem("dv"), ctx.var_done)
            h, v = ctx.inc("ac")
            eng.activation(s_lnv[:], s_var[:], AF.Ln).then_inc(h, 1)
            eng.wait_ge(h, v)
            h, v = ctx.inc("ac")
            eng.activation(s_rstd[:], s_lnv[:], AF.Exp,
                           scale=-0.5).then_inc(h, 1)
            ctx.rstd_done = v

        lxb_t = [0] * NT
        dvb = ctx.val("dv")
        ap_done = [dvb + 3 * t + 3 for t in range(NT)]
        pe_b = ctx.val("pe")
        ac_b = ctx.val("ac")

        @block.sync
        def _(eng):
            for t in range(NT):
                if t < 2:
                    eng.wait_ge(ctx.sem("ac"), sq_done[NT - 2 + t])
                    eng.wait_ge(ctx.sem("dv"), sum_done[NT - 2 + t])
                else:
                    eng.wait_ge(ctx.sem("dv"), ap_done[t - 2])
                lxb_t[t] = dma(eng, ctx, "lx", s_xt[:, t % 2, :],
                               xsrc[:, t, :])

        @block.vector
        def _(eng):
            eng.wait_ge(ctx.sem("ac"), ctx.rstd_done)
            for t in range(NT):
                eng.wait_ge(ctx.sem("lx"), lxb_t[t])
                if t >= 2:
                    eng.wait_ge(ctx.sem("pe"), pe_b + 16 * (t - 1))
                h, v = ctx.inc("dv")
                eng.tensor_scalar(s_xf[:], s_xt[:, t % 2, :],
                                  s_mean[:, t:t + 1], s_rstd[:, t:t + 1],
                                  ALU.subtract, ALU.mult).then_inc(h, 1)
                eng.wait_ge(h, v)
                h, v = ctx.inc("dv")
                eng.tensor_copy(s_xnp[:, t % 2, :], s_xf[:]).then_inc(h, 1)
                eng.wait_ge(h, v)
                h, v = ctx.inc("dv")
                eng.tensor_tensor(s_xlp[:, t % 2, :], s_xf[:],
                                  s_xnp[:, t % 2, :],
                                  ALU.subtract).then_inc(h, 1)

        rtr_done = [0] * NT
        dv_rtbase = None

        @block.tensor
        def _(eng):
            k = 0
            for t in range(NT):
                eng.wait_ge(ctx.sem("dv"), ap_done[t])
                if t >= 2:
                    eng.wait_ge(ctx.sem("ac"), ac_b + 16 * (t - 1))
                for c in range(8):
                    if k >= 2:
                        eng.wait_ge(ctx.sem("ac"), ac_b + k - 1)
                    h, v = ctx.inc("pe")
                    eng.transpose(p_tr[k % 2][:],
                                  s_xnp[:, t % 2, 128 * c:128 * (c + 1)],
                                  s_idb[:]).then_inc(h, 1)
                    k += 1
                for c in range(8):
                    if k >= 2:
                        eng.wait_ge(ctx.sem("ac"), ac_b + k - 1)
                    h, v = ctx.inc("pe")
                    eng.transpose(p_tr[k % 2][:],
                                  s_xlp[:, t % 2, 128 * c:128 * (c + 1)],
                                  s_idb[:]).then_inc(h, 1)
                    k += 1

        @block.scalar
        def _(eng):
            k = 0
            for t in range(NT):
                for c in range(8):
                    eng.wait_ge(ctx.sem("pe"), pe_b + k + 1)
                    h, v = ctx.inc("ac")
                    eng.activation(s_xnT[:, c, 128 * t:128 * (t + 1)],
                                   p_tr[k % 2][:], AF.Copy).then_inc(h, 1)
                    k += 1
                for c in range(8):
                    eng.wait_ge(ctx.sem("pe"), pe_b + k + 1)
                    h, v = ctx.inc("ac")
                    eng.activation(s_xloT[:, t % 2, c, :],
                                   p_tr[k % 2][:], AF.Copy).then_inc(h, 1)
                    k += 1
            ctx.xnT_done = ctx.val("ac")

        tile_tr_done = [ac_b + 16 * (t + 1) for t in range(NT)]
        pe_r = ctx.val("pe")
        rmm_done = [pe_r + t + 1 for t in range(NT)]
        for _ in range(NT):
            ctx.inc("pe")
        dv_g = ctx.val("dv")
        lg_read = [dv_g + 11 * t + 1 for t in range(NT)]
        keep_done = [dv_g + 11 * t + 7 for t in range(NT)]
        g_done = [dv_g + 11 * (t + 1) for t in range(NT)]
        for _ in range(11 * NT):
            ctx.inc("dv")
        ac_e = ctx.val("ac")
        exp_done = [ac_e + t + 1 for t in range(NT)]
        for _ in range(NT):
            ctx.inc("ac")

        @block.tensor
        def _(eng, lg_read=lg_read):
            eng.wait_ge(ctx.sem("dv"), ctx.rw_done)
            for t in range(NT):
                eng.wait_ge(ctx.sem("ac"), tile_tr_done[t])
                if t >= 1:
                    eng.wait_ge(ctx.sem("dv"), lg_read[t - 1])
                first = True
                for c in range(8):
                    eng.matmul(p_r[:], s_xnT[:, c, 128 * t:128 * (t + 1)],
                               s_rwh[:, c, :], start=first, stop=False)
                    first = False
                for c in range(8):
                    eng.matmul(p_r[:], s_xnT[:, c, 128 * t:128 * (t + 1)],
                               s_rwl[:, c, :], start=False, stop=False)
                for c in range(8):
                    i = eng.matmul(p_r[:], s_xloT[:, t % 2, c, :],
                                   s_rwh[:, c, :], start=False,
                                   stop=(c == 7))
                i.then_inc(ctx.sem("pe"), 1)

        sem_dvh = ctx.sem("dv")

        @block.vector
        def _(eng, rmm_done=rmm_done, exp_done=exp_done):
            cnt = dv_g
            for t in range(NT):
                eng.wait_ge(ctx.sem("pe"), rmm_done[t])

                def step(i):
                    nonlocal cnt
                    cnt += 1
                    i.then_inc(sem_dvh, 1)
                    eng.wait_ge(sem_dvh, cnt)

                step(eng.tensor_tensor(s_lg[:], p_r[:], s_rbb[:], ALU.add))
                step(eng.reduce_max(s_m1[:], s_lg[:], axis=AX.X))
                step(eng.tensor_scalar(s_nm1[:], s_m1[:], -1.0, None,
                                       ALU.mult))
                step(eng.tensor_scalar(s_meq[:], s_lg[:], s_m1[:, 0:1],
                                       None, ALU.is_ge))
                step(eng.scalar_tensor_tensor(s_lg2[:], s_meq[:], -1e30,
                                              s_lg[:], ALU.mult, ALU.add))
                step(eng.reduce_max(s_m2[:], s_lg2[:], axis=AX.X))
                step(eng.tensor_scalar(s_keep[:], s_lg[:], s_m2[:, 0:1],
                                       None, ALU.is_ge))
                eng.wait_ge(ctx.sem("ac"), exp_done[t])
                step(eng.tensor_tensor(s_ex[:], s_ex[:], s_keep[:],
                                       ALU.mult))
                step(eng.reduce_sum(s_sm[:], s_ex[:], axis=AX.X))
                step(eng.reciprocal(s_iv[:], s_sm[:]))
                step(eng.tensor_scalar(s_g[:, sq, t, :], s_ex[:],
                                       s_iv[:, 0:1], None, ALU.mult))
            assert cnt == dv_g + 11 * NT

        @block.scalar
        def _(eng, keep_done=keep_done):
            for t in range(NT):
                eng.wait_ge(ctx.sem("dv"), keep_done[t])
                eng.activation(s_ex[:], s_lg[:], AF.Exp,
                               bias=s_nm1[:, 0:1]).then_inc(ctx.sem("ac"), 1)

        @block.sync
        def _(eng):
            eng.wait_ge(ctx.sem("ac"), ctx.xnT_done)
            dma(eng, ctx, "st", d_xnT[sq], s_xnT[:])
        st_prev = ctx.val("st")

    ln_done_st = ctx.val("st")
    ln_done_dv = ctx.val("dv")
    es_ln.close()

    pe_ln_done = ctx.val("pe")
    ctx.es = es_x = ExitStack()
    sW1 = ctx.S("w1b", [128, 8, HF], BF16)
    sW2 = ctx.S("w2b", [128, 32, D], BF16)
    s_h = ctx.S("hT", [128, 32, 512], BF16)
    s_xnt = ctx.S("xnt", [128, 8, 512], BF16)
    s_b1 = ctx.S("b1s", [128, 32], F32)
    s_b2 = ctx.S("b2s", [128, 1024], F32)
    s_acc = ctx.S("acc", [128, 2, 1024], F32)
    s_x3 = ctx.S("x3t", [128, 2, 1024], F32)
    s_t1 = ctx.S("t1", [128, 512], F32)
    p_m = [ctx.P("pm%d" % i, [128, 512], F32) for i in range(4)]
    ctx.es = ctx_es_saved

    NTT = 4
    w1_free_pe = 0
    w2_free_pe = 0
    ev_prev = 0
    mm1_prev = 0
    last_acc_st = {}

    for e in range(NE):
        @block.gpsimd
        def _(eng, e=e, w1_free=w1_free_pe, w2_free=w2_free_pe,
              ev_prev=ev_prev):
            if e == 0:
                eng.wait_ge(ctx.sem("st"), ln_done_st)
                eng.wait_ge(ctx.sem("dv"), ln_done_dv)
                eng.wait_ge(ctx.sem("pe"), pe_ln_done)
            else:
                eng.wait_ge(ctx.sem("pe"), w1_free)
            for c in range(8):
                dma(eng, ctx, "lw", sW1[:, c, :],
                    d_W["w1"][e, 128 * c:128 * (c + 1), :])
            ctx.cnt["_w1ld_%d" % e] = ctx.val("lw")
            if e > 0:
                eng.wait_ge(ctx.sem("pe"), w2_free)
                eng.wait_ge(ctx.sem("dv"), ev_prev)
            for c in range(32):
                dma(eng, ctx, "lw", sW2[:, c, :],
                    d_W["w2"][e, 128 * c:128 * (c + 1), :])
            dma(eng, ctx, "lw", s_b1[:], d_W["b1c"][e])
            dma(eng, ctx, "lw", s_b2[:], d_W["b2b"][e])
            ctx.cnt["_w2ld_%d" % e] = ctx.val("lw")

        for sq in range(NSEQ):
            for tt in range(NTT):
                tok0 = 512 * tt

                @block.sync
                def _(eng, e=e, sq=sq, tt=tt, mm1_prev=mm1_prev):
                    if mm1_prev:
                        eng.wait_ge(ctx.sem("pe"), mm1_prev)
                    ctx.cnt["_xl"] = dma(eng, ctx, "lx", s_xnt[:],
                                         d_xnT[sq][:, :, tok0:tok0 + 512])
                xl = ctx.cnt["_xl"]

                pe_m1 = ctx.val("pe")
                ac_m1 = ctx.val("ac")

                @block.tensor
                def _(eng, e=e, sq=sq, tt=tt, pe_m1=pe_m1, ac_m1=ac_m1,
                      xl=xl):
                    eng.wait_ge(ctx.sem("lx"), xl)
                    if (sq, tt) == (0, 0):
                        eng.wait_ge(ctx.sem("lw"), ctx.cnt["_w1ld_%d" % e])
                    for hc in range(32):
                        if hc >= 2:
                            eng.wait_ge(ctx.sem("ac"), ac_m1 + hc - 1)
                        for c in range(8):
                            i = eng.matmul(p_m[hc % 2][:],
                                           sW1[:, c, 128 * hc:128 * (hc + 1)],
                                           s_xnt[:, c, :],
                                           start=(c == 0), stop=(c == 7))
                        h, v = ctx.inc("pe")
                        i.then_inc(h, 1)
                mm1_prev = ctx.val("pe")

                @block.scalar
                def _(eng, e=e, sq=sq, tt=tt, pe_m1=pe_m1):
                    if (sq, tt) == (0, 0):
                        eng.wait_ge(ctx.sem("lw"), ctx.cnt["_w2ld_%d" % e])
                    for hc in range(32):
                        eng.wait_ge(ctx.sem("pe"), pe_m1 + hc + 1)
                        h, v = ctx.inc("ac")
                        eng.activation(s_h[:, hc, :], p_m[hc % 2][:],
                                       AF.Relu,
                                       bias=s_b1[:, hc:hc + 1]).then_inc(h, 1)
                relu_done = ctx.val("ac")

                OPS = 2 if e < NE - 1 else 3
                pe_m2 = ctx.val("pe")
                dv_ev = ctx.val("dv")
                acl = {}
                x3l = {}

                st_base = ctx.val("st")
                st_sub = [st_base + 16 * (k + 1) for k in range(4)]
                slot_prev = dict(ctx.cnt.get("_slotst", {0: 0, 1: 0}))

                @block.sync
                def _(eng, e=e, sq=sq, tt=tt, dv_ev=dv_ev, OPS=OPS,
                      slot_prev=slot_prev, st_sub=st_sub):
                    def emit_store(sub):
                        eng.wait_ge(ctx.sem("dv"),
                                    dv_ev + 2 * OPS * (sub + 1))
                        if e == NE - 1:
                            stv = dma(eng, ctx, "st",
                                      d_xout[sq].rearrange(
                                          "(t p) d -> p t d",
                                          p=128)[:, 4 * tt + sub, :],
                                      s_acc[:, sub % 2, :])
                        else:
                            stv = dma(eng, ctx, "st",
                                      d_acc[sq][:, 4 * tt + sub, :],
                                      s_acc[:, sub % 2, :])
                        assert stv == st_sub[sub], (stv, st_sub, sub)
                        last_acc_st[(sq, tt, sub)] = stv

                    for sub in range(4):
                        if sub >= 2:
                            emit_store(sub - 2)
                            eng.wait_ge(ctx.sem("st"), st_sub[sub - 2])
                            eng.wait_ge(ctx.sem("dv"),
                                        dv_ev + 2 * OPS * (sub - 1))
                        else:
                            eng.wait_ge(ctx.sem("st"), slot_prev[sub % 2])
                        if e > 0:
                            eng.wait_ge(ctx.sem("st"),
                                        last_acc_st.get((sq, tt, sub), 0))
                            acl[sub] = dma(eng, ctx, "lx",
                                           s_acc[:, sub % 2, :],
                                           d_acc[sq][:, 4 * tt + sub, :])
                        if e == NE - 1:
                            x3l[sub] = dma(
                                eng, ctx, "lx", s_x3[:, sub % 2, :],
                                d_xin[sq].rearrange("(t p) d -> p t d",
                                                    p=128)[:, 4 * tt + sub, :])
                    emit_store(2)
                    emit_store(3)
                ctx.cnt["_slotst"] = {0: st_sub[2], 1: st_sub[3]}

                @block.tensor
                def _(eng, e=e, sq=sq, tt=tt, relu_done=relu_done,
                      dv_ev=dv_ev, OPS=OPS):
                    eng.wait_ge(ctx.sem("ac"), relu_done)
                    kk = 0
                    for sub in range(4):
                        for o2 in range(2):
                            if kk >= 2:
                                eng.wait_ge(ctx.sem("dv"),
                                            dv_ev + OPS * (kk - 2) + 1)
                            for hc in range(32):
                                i = eng.matmul(
                                    p_m[2 + kk % 2][:],
                                    s_h[:, hc, 128 * sub:128 * (sub + 1)],
                                    sW2[:, hc, 512 * o2:512 * (o2 + 1)],
                                    start=(hc == 0), stop=(hc == 31))
                            h, v = ctx.inc("pe")
                            i.then_inc(h, 1)
                            kk += 1
                mm2_done = ctx.val("pe")

                @block.vector
                def _(eng, e=e, sq=sq, tt=tt, pe_m2=pe_m2, dv_ev=dv_ev,
                      OPS=OPS, slot_prev=slot_prev, st_sub=st_sub):
                    kk = 0
                    for sub in range(4):
                        for o2 in range(2):
                            eng.wait_ge(ctx.sem("pe"), pe_m2 + kk + 1)
                            if e == 0 and o2 == 0:
                                eng.wait_ge(ctx.sem("st"),
                                            st_sub[sub - 2] if sub >= 2
                                            else slot_prev[sub % 2])
                            if e > 0:
                                eng.wait_ge(ctx.sem("lx"), acl[sub])
                            if e == NE - 1:
                                eng.wait_ge(ctx.sem("lx"), x3l[sub])
                            g_col = s_g[:, sq, 4 * tt + sub, e:e + 1]
                            h, v = ctx.inc("dv")
                            eng.tensor_tensor(
                                s_t1[:], p_m[2 + kk % 2][:],
                                s_b2[:, 512 * o2:512 * (o2 + 1)],
                                ALU.add).then_inc(h, 1)
                            eng.wait_ge(h, v)
                            h, v = ctx.inc("dv")
                            dst = s_acc[:, sub % 2, 512 * o2:512 * (o2 + 1)]
                            if e == 0:
                                eng.tensor_scalar(dst, s_t1[:], g_col, None,
                                                  ALU.mult).then_inc(h, 1)
                            else:
                                eng.scalar_tensor_tensor(
                                    dst, s_t1[:], g_col, dst,
                                    ALU.mult, ALU.add).then_inc(h, 1)
                            if e == NE - 1:
                                eng.wait_ge(h, v)
                                h, v = ctx.inc("dv")
                                eng.tensor_tensor(
                                    dst, dst,
                                    s_x3[:, sub % 2,
                                         512 * o2:512 * (o2 + 1)],
                                    ALU.add).then_inc(h, 1)
                            kk += 1
                ev_done = ctx.val("dv")

                ev_prev = ev_done
        w1_free_pe = mm1_prev
        w2_free_pe = mm2_done

    es_x.close()

    @block.sync
    def _(eng):
        eng.wait_ge(ctx.sem("st"), ctx.val("st"))


def build_l1moe():
    nc = bass.Bass("TRN2", target_bir_lowering=False)
    d_xin = nc.dram_tensor("xin", [NSEQ, S, D], F32, kind="ExternalInput").ap()
    d_W = {
        "rw": nc.dram_tensor("rw", [D, NE], F32, kind="ExternalInput").ap(),
        "rbb": nc.dram_tensor("rbb", [128, NE], F32,
                              kind="ExternalInput").ap(),
        "w1": nc.dram_tensor("w1", [NE, D, HF], F32,
                             kind="ExternalInput").ap(),
        "b1c": nc.dram_tensor("b1c", [NE, 128, 32], F32,
                              kind="ExternalInput").ap(),
        "w2": nc.dram_tensor("w2", [NE, HF, D], F32,
                             kind="ExternalInput").ap(),
        "b2b": nc.dram_tensor("b2b", [NE, 128, D], F32,
                              kind="ExternalInput").ap(),
    }
    d_eye = nc.dram_tensor("eye", [128, 128], F32, kind="ExternalInput").ap()
    d_xout = nc.dram_tensor("xout", [NSEQ, S, D], F32,
                            kind="ExternalOutput").ap()
    d_xnT = nc.dram_tensor("xnTs", [NSEQ, 128, 8, S], BF16,
                           kind="Internal").ap()
    d_acc = nc.dram_tensor("accs", [NSEQ, 128, NT, D], F32,
                           kind="Internal").ap()
    es = ExitStack()
    ctx = Ctx(nc, es)
    block = es.enter_context(nc.Block())
    emit_l1moe(ctx, block, d_xin, d_W, d_eye, d_xout, d_xnT, d_acc)
    es.close()
    return nc


def moe_host_layouts(rb, b1, b2):
    rbb = np.ascontiguousarray(np.broadcast_to(rb[None, :], (128, NE)))
    b1c = np.ascontiguousarray(
        b1.reshape(NE, 32, 128).transpose(0, 2, 1))
    b2b = np.ascontiguousarray(
        np.broadcast_to(b2[:, None, :], (NE, 128, D)))
    return rbb, b1c, b2b


def emit_l0ffn(ctx, block, d_xin, d_w, d_eye, d_xout):
    nc = ctx.nc
    s_idb = ctx.S("idb", [128, 128], BF16)
    s_wb = ctx.S("wb", [128, 8, 1024], BF16)
    s_sgn = ctx.S("sgn", [128, 8, 1024], BF16)
    lnb = make_ln_bufs(ctx)
    s_WsT = ctx.S("WsT", [128, 8, 1024], BF16)
    s_x = ctx.S("x", [128, NT, 1024], F32)
    s_xn = ctx.S("xn", [128, NT, 1024], BF16)
    s_xnT = ctx.S("xnT", [128, 8, S], BF16)
    p_mm = [ctx.P("pmm%d" % i, [128, 512], F32) for i in range(2)]
    p_tr = [ctx.P("ptr%d" % i, [128, 128], BF16) for i in range(4)]

    @block.gpsimd
    def _(eng):
        dma(eng, ctx, "lw", s_idb[:], d_eye)
        dma(eng, ctx, "lw", s_wb[:], d_w.rearrange("(c p) i -> p c i", p=128))
        ctx.w_loaded = ctx.val("lw")

    @block.scalar
    def _(eng):
        eng.wait_ge(ctx.sem("lw"), ctx.w_loaded)
        for c in range(8):
            i = eng.activation(s_sgn[:, c, :], s_wb[:, c, :], AF.Sign)
        h, v = ctx.inc("ac")
        i.then_inc(h, 1)
        ctx.sgn_done = v

    jobs = []
    for c in range(8):
        for t in range(8):
            jobs.append((s_sgn[:, c, 128 * t:128 * (t + 1)],
                         s_WsT[:, t, 128 * c:128 * (c + 1)]))
    emit_transpose_batch(ctx, block, jobs, s_idb[:], p_tr,
                         wait=[("ac", ctx.sgn_done), ("lw", ctx.val("lw"))])

    st_prev = 0
    for sq in range(NSEQ):
        @block.sync
        def _(eng):
            eng.wait_ge(ctx.sem("st"), st_prev)
            dma(eng, ctx, "lx", s_x[:],
                d_xin[sq].rearrange("(t p) d -> p t d", p=128))
        lx_val = ctx.val("lx")

        dv_ln = emit_ln(ctx, block, lnb, s_x, s_xn, lx_val)

        jobs = []
        for t in range(NT):
            for c in range(8):
                jobs.append((s_xn[:, t, 128 * c:128 * (c + 1)],
                             s_xnT[:, c, 128 * t:128 * (t + 1)]))
        dv_tr = emit_transpose_batch(ctx, block, jobs, s_idb[:], p_tr,
                                     wait=[("dv", dv_ln)])

        pe_base = ctx.val("pe")
        dv_base = ctx.val("dv")

        @block.tensor
        def _(eng):
            eng.wait_ge(ctx.sem("dv"), dv_tr)
            k = 0
            for t in range(NT):
                for o2 in range(2):
                    if k >= 2:
                        eng.wait_ge(ctx.sem("dv"), dv_base + k - 1)
                    for ic in range(8):
                        i = eng.matmul(p_mm[k % 2][:],
                                       s_xnT[:, ic, 128 * t:128 * (t + 1)],
                                       s_WsT[:, ic, 512 * o2:512 * (o2 + 1)],
                                       start=(ic == 0), stop=(ic == 7))
                    h, v = ctx.inc("pe")
                    i.then_inc(h, 1)
                    k += 1

        @block.vector
        def _(eng):
            k = 0
            for t in range(NT):
                for o2 in range(2):
                    eng.wait_ge(ctx.sem("pe"), pe_base + k + 1)
                    h, v = ctx.inc("dv")
                    eng.tensor_tensor(s_x[:, t, 512 * o2:512 * (o2 + 1)],
                                      s_x[:, t, 512 * o2:512 * (o2 + 1)],
                                      p_mm[k % 2][:],
                                      ALU.add).then_inc(h, 1)
                    k += 1
            ctx.res_done = ctx.val("dv")

        @block.sync
        def _(eng):
            eng.wait_ge(ctx.sem("dv"), ctx.res_done)
            dma(eng, ctx, "st", d_xout[sq].rearrange("(t p) d -> p t d", p=128),
                s_x[:])
        st_prev = ctx.val("st")

    @block.sync
    def _(eng):
        eng.wait_ge(ctx.sem("st"), ctx.val("st"))


def build_l0ffn():
    nc = bass.Bass("TRN2", target_bir_lowering=False)
    d_xin = nc.dram_tensor("xin", [NSEQ, S, D], F32, kind="ExternalInput").ap()
    d_w = nc.dram_tensor("wffn", [D, D], F32, kind="ExternalInput").ap()
    d_eye = nc.dram_tensor("eye", [128, 128], F32, kind="ExternalInput").ap()
    d_xout = nc.dram_tensor("xout", [NSEQ, S, D], F32,
                            kind="ExternalOutput").ap()
    es = ExitStack()
    ctx = Ctx(nc, es)
    block = es.enter_context(nc.Block())
    emit_l0ffn(ctx, block, d_xin, d_w, d_eye, d_xout)
    es.close()
    return nc


EYE = np.eye(128, dtype=np.float32)

_prog_cache = {}


def _get_progs():
    if "phased" not in _prog_cache:
        _prog_cache["phased"] = (build_l0attn(), build_l0ffn(),
                                 build_l1attn(), build_l1moe())
    return _prog_cache["phased"]


def kernel(**inputs):
    x = np.ascontiguousarray(inputs["L0_attn_x"], dtype=np.float32)
    W = {k: np.ascontiguousarray(v, dtype=np.float32)
         for k, v in inputs.items() if k != "L0_attn_x"}
    masks = make_masks()
    rbb, b1c, b2b = moe_host_layouts(W["L1_ffn_router_b"], W["L1_ffn_b1"],
                                     W["L1_ffn_b2"])
    nc_a, nc_f, nc_b, nc_m = _get_progs()
    cores = list(range(NCORES))
    shards = [np.ascontiguousarray(x[NSEQ * c:NSEQ * (c + 1)])
              for c in cores]

    def launch(nc, maps):
        res = run_bass_kernel_spmd(nc, maps, core_ids=cores)
        return [r["xout"] for r in res.results]

    x1 = launch(nc_a, [
        {"xin": shards[c], "wdq": W["L0_attn_W_dq"],
         "wuq": W["L0_attn_W_uq"], "wdkv": W["L0_attn_W_dkv"],
         "wuk": W["L0_attn_W_uk"], "wuv": W["L0_attn_W_uv"],
         "wo": W["L0_attn_W_o"], "eye": EYE, "mask": masks,
         "ones": ONES_ROW} for c in cores])
    x2 = launch(nc_f, [
        {"xin": x1[c], "wffn": W["L0_ffn_weight"], "eye": EYE}
        for c in cores])
    x3 = launch(nc_b, [
        {"xin": x2[c], "wq": W["L1_attn_Wq"], "wk": W["L1_attn_Wk"],
         "wv": W["L1_attn_Wv"], "eye": EYE, "ones": ONES_ROW}
        for c in cores])
    xo = launch(nc_m, [
        {"xin": x3[c], "rw": W["L1_ffn_router_w"], "rbb": rbb,
         "w1": W["L1_ffn_w1"], "b1c": b1c, "w2": W["L1_ffn_w2"],
         "b2b": b2b, "eye": EYE} for c in cores])
    return np.concatenate(xo, axis=0)
